# version 01
# baseline (speedup 1.0000x reference)
import numpy as np
import ml_dtypes
import concourse.bass as bass
import concourse.mybir as mybir
from concourse.bass_utils import run_bass_kernel_spmd

F32 = mybir.dt.float32
F32R = mybir.dt.float32r
BF16 = mybir.dt.bfloat16
U8 = mybir.dt.uint8
AF = mybir.ActivationFunctionType
ALU = mybir.AluOpType
AX = mybir.AxisListType

NCORES = 8
T, NT, D, DC = 2048, 16, 1024, 8
H = 16
E, FC = 8, 32
NTH = 8
EPS_LN, EPS_EXP, EPS_ATT = 1e-5, 1e-12, 1e-6

ENGINES = ("sync", "tensor", "vector", "scalar", "gpsimd")


class Ctr:
    def __init__(self, nc, name, async_=False):
        self.h = nc.alloc_semaphore(name)
        self.n = 0
        self.async_ = async_

    def snap(self):
        return (self, self.n)


class Sch:

    def __init__(self, nc):
        self.nc = nc
        self.items = {e: [] for e in ENGINES}
        self._done = {e: {} for e in ENGINES}
        self._inc_engines = {}

    def on(self, eng, fn, waits=(), inc=None, inc_by=1):
        w = []
        for wv in waits:
            if wv is None:
                continue
            c, v = wv
            if v <= 0 or self._done[eng].get(id(c), -1) >= v:
                continue
            if not c.async_ and self._inc_engines.get(id(c)) == {eng}:
                self._done[eng][id(c)] = v
                continue
            w.append((c, v))
            self._done[eng][id(c)] = v
        self.items[eng].append((w, fn, inc, inc_by))
        if inc is not None:
            inc.n += inc_by
            self._inc_engines.setdefault(id(inc), set()).add(eng)

    def build(self, block):
        def mk(items):
            def body(e):
                for w, fn, inc, by in items:
                    for c, v in w:
                        e.wait_ge(c.h, v)
                    ins = fn(e)
                    if inc is not None:
                        ins.then_inc(inc.h, by)
            return body
        for eng in ENGINES:
            getattr(block, eng)(mk(self.items[eng]))


class Arena:
    ESZ = {F32: 4, F32R: 4, BF16: 2, U8: 1}

    def __init__(self, nc, limit):
        self.nc = nc
        self.limit = limit
        self.base = nc.alloc_sbuf_tensor("arena", [128, limit], U8)
        self.addr = nc.lookup_mloc(self.base).addr
        self.off = 0
        self.hi = 0

    def _mk(self, name, n_elems, dtype, byte_off):
        t = self.nc.alloc_sbuf_tensor_at(name, [128, n_elems], dtype,
                                         offset=self.addr + byte_off)
        return t[:, :]

    def carve(self, name, n_elems, dtype):
        nbytes = n_elems * self.ESZ[dtype]
        o = (self.off + 31) & ~31
        assert o + nbytes <= self.limit, (name, o, nbytes)
        self.off = o + nbytes
        self.hi = max(self.hi, self.off)
        return self._mk(name, n_elems, dtype, o)

    def at(self, off, n_elems, dtype, name):
        nbytes = n_elems * self.ESZ[dtype]
        assert off + nbytes <= self.limit
        self.hi = max(self.hi, off + nbytes)
        return self._mk(name, n_elems, dtype, off)

    def mark(self):
        return self.off

    def reset(self, off):
        self.off = off


def build_program(taps=()):
    nc = bass.Bass("TRN2", target_bir_lowering=False, debug=False)

    def din(name, shape):
        return nc.dram_tensor(name, shape, F32, kind="ExternalInput").ap()

    x_h = din("x", [T, D])
    wq_h, wk_h, wv_h, wo_h = (din(n, [D, D]) for n in ("wq", "wk", "wv", "wo"))
    bq_h, bk_h, bv_h, bo_h = (din(n, [D]) for n in ("bq", "bk", "bv", "bo"))
    wr_h, br_h = din("wr", [D, E]), din("br", [E])
    wi_h, wg_h = din("wi", [E, D, 4096]), din("wg", [E, D, 4096])
    bi_h, bg_h = din("bi", [E, 4096]), din("bg", [E, 4096])
    woe_h, boe_h = din("woe", [E, 4096, D]), din("boe", [E, D])
    gam_h, bet_h = din("gam", [E, D]), din("bet", [E, D])
    out_h = nc.dram_tensor("out", [T, D], F32, kind="ExternalOutput").ap()
    spill_h = nc.dram_tensor("y1spill", [128, NT * D], F32, kind="Internal").ap()
    spill3 = spill_h.rearrange("p (n d) -> p n d", n=NT)
    cc_in_h = nc.dram_tensor("cc_in", [128, 520], F32, kind="Internal")
    cc_out_h = nc.dram_tensor("cc_out", [128, 520], F32, kind="Internal")
    tap_h = {}
    for tname in taps:
        shp = {"y1": ([128, NT * D], F32), "xf": ([128, DC * NTH * 128], F32),
               "c": ([128, NTH * E], F32), "h01": ([128, 2048], F32),
               "oacc": ([128, NTH * D], F32), "ln1": ([128, DC * T], BF16),
               "q": ([128, DC * T], BF16), "aro": ([128, 520], F32),
               "st": ([128, NT * 4], F32), "lt": ([128, D], BF16),
               "xtm1": ([128, NTH * D], BF16), "oacc1": ([128, NTH * D], F32),
               "c1": ([128, NTH * E], F32)}[tname]
        tap_h[tname] = nc.dram_tensor("tap_" + tname, shp[0], shp[1],
                                      kind="ExternalOutput").ap()

    id_bf_h = nc.inline_tensor(np.eye(128, dtype=ml_dtypes.bfloat16), "id_bf")
    id_f32_h = nc.inline_tensor(np.eye(128, dtype=np.float32), "id_f32")

    PZ0 = nc.alloc_psum_tensor("PZ0", [128, 1024], F32)
    PZ1 = nc.alloc_psum_tensor("PZ1", [128, 1024], F32)
    PR0 = nc.alloc_psum_tensor("PR0", [128, 1024], F32)
    PR1 = nc.alloc_psum_tensor("PR1", [128, 1024], F32)
    tp_bf = PR1[:].bitcast(BF16)
    tp_f32 = PR1
    tp_f32r = PR1[:].bitcast(F32R)

    A = Arena(nc, 212700 if taps else 212000)
    idb = A.carve("idb", 128, BF16)
    idf = A.carve("idf", 128, F32)
    idr = A.carve("idr", 128, F32R)
    ones_r_bf = A.carve("ones_r_bf", 128, BF16)
    ones_c_bf = A.carve("ones_c_bf", 1, BF16)
    ones_r_f32 = A.carve("ones_r_f32", 128, F32)
    ones_r_r = A.carve("ones_r_r", 128, F32R)
    eps_ln = A.carve("eps_ln", 1, F32)
    eps_exp = A.carve("eps_exp", 1, F32)
    rowtmp = A.carve("rowtmp", D, F32)
    rowo_bf = A.carve("rowo_bf", D, BF16)
    wr_sb = A.carve("wr_sb", DC * E, F32R)
    wr3 = wr_sb.rearrange("p (c r) -> p c r", c=DC)
    br_sb = A.carve("br_sb", E, F32R)
    beta_sb = A.carve("beta_sb", D, F32)
    cT_sb = A.carve("cT_sb", 128, F32)
    b32 = A.carve("b32", 128, F32)
    bqf = A.carve("bqf", DC, F32)
    nbqf = A.carve("nbqf", DC, F32)
    bif = [A.carve(f"bif{i}", FC, F32) for i in range(2)]
    bgf = [A.carve(f"bgf{i}", FC, F32) for i in range(2)]
    bnst = A.carve("bnst", 12, F32)
    stat = A.carve("stat", 8, F32)
    ln2st = A.carve("ln2st", 2 * NTH, F32)
    lgts = A.carve("lgts", NTH * E, F32)
    mkts = A.carve("mkts", NTH * E, F32)
    m1s = A.carve("m1s", NTH, F32)
    m2s = A.carve("m2s", NTH, F32)
    eq1v = A.carve("eq1v", NTH * E, F32)
    eq2v = A.carve("eq2v", NTH * E, F32)
    ddv = A.carve("ddv", NTH, F32)
    w1v = A.carve("w1v", NTH, F32)
    w2v = A.carve("w2v", NTH, F32)
    c_sb = A.carve("c_sb", NTH * E, F32)
    c3 = c_sb.rearrange("p (n e) -> p n e", n=NTH)
    nrm = A.carve("nrm", 16, F32)
    rnn = A.carve("rnn", 16, F32)
    sttap = A.carve("sttap", NT * 4, F32) if "st" in taps else None
    y1 = A.carve("y1", NT * D, F32)
    y1v = y1.rearrange("p (n d) -> p n d", n=NT)
    slots_bf = y1.bitcast(BF16)

    def hview(fc):
        return slots_bf[:, fc * 1024:(fc + 1) * 1024]

    perm_end = A.mark()

    ln1f = A.carve("ln1f", DC * T, BF16)
    ln1f3 = ln1f.rearrange("p (c t) -> p c t", c=DC)
    q3r = A.carve("q3", DC * T, BF16)
    q3 = q3r.rearrange("p (c t) -> p c t", c=DC)
    wkb = A.carve("wkb", DC * D, BF16)
    wkb3 = wkb.rearrange("p (c d) -> p c d", c=DC)
    wvb = A.carve("wvb", DC * D, BF16)
    wvb3 = wvb.rearrange("p (c d) -> p c d", c=DC)
    wstage = [A.carve(f"wstage{i}", D, F32) for i in range(2)]
    ktv = A.carve("ktv", D, BF16)
    vtv = A.carve("vtv", D, BF16)
    sa = A.carve("sa", 520, F32)
    se = A.carve("se", D, F32)
    lt_bf = A.carve("lt_bf", D, BF16)
    AT0 = (A.mark() + 31) & ~31
    at_bf = A.carve("at_bf", D, BF16)
    kvacc = A.at(AT0, 512, F32, "kvacc")
    kvd = A.carve("kvd", DC * 128, BF16)
    kvd3 = kvd.rearrange("p (c m) -> p c m", c=DC)
    ksc = A.carve("ksc", 16, BF16)
    rowk_bf = A.carve("rowk_bf", D, BF16)
    rowv_bf = A.carve("rowv_bf", D, BF16)
    aro = A.carve("aro", 520, F32)
    regA_end = A.mark()

    A.reset(perm_end)
    xf_sb = A.carve("xf_sb", DC * NTH * 128, F32R)
    xf3 = xf_sb.rearrange("p (c t) -> p c t", c=DC)
    oa = A.carve("oa", NTH * D, F32)
    oa3 = oa.rearrange("p (n d) -> p n d", n=NTH)
    xtm = A.carve("xtm", NTH * D, BF16)
    xtm3 = xtm.rearrange("p (n d) -> p n d", n=NTH)
    wgi0 = A.carve("wgi0", DC * 256, F32R)
    U0 = A.mark()
    UN = A.carve("UNION", 6 * D, F32)
    wgi_u = A.at(U0, DC * 256, F32R, "wgi_u")
    wgi3 = [wgi0.rearrange("p (c f) -> p c f", c=DC),
            wgi_u.rearrange("p (c f) -> p c f", c=DC)]
    wstg = [A.at(U0, 2 * D, F32, "wstg0"), A.at(U0 + 8192, 2 * D, F32, "wstg1")]
    wobf = [A.at(U0 + 16384, 2 * D, BF16, "wobf0"), A.at(U0 + 20480, 2 * D, BF16, "wobf1")]
    wobf3 = [w.rearrange("p (c d) -> p c d", c=2) for w in wobf]
    ringA = A.at(U0, D, F32, "ringA")
    ringB = A.at(U0 + 4096, D, F32, "ringB")
    gbc = A.carve("gbc", D, F32)
    esc = A.carve("esc", D, F32)
    SUB0 = A.mark()
    subreg = A.carve("subreg", D, F32)
    ltr = A.at(SUB0, D, F32R, "ltr")
    t1_bf = A.at(SUB0, D, BF16, "t1_bf")
    regC_end = A.mark()

    sbuf_hi = A.hi

    S = Sch(nc)
    dY = Ctr(nc, "dY", async_=True)
    dS = Ctr(nc, "dS", async_=True)
    dG = Ctr(nc, "dG", async_=True)
    sT = Ctr(nc, "sT")
    sV = Ctr(nc, "sV")
    sS = Ctr(nc, "sS")
    sG = Ctr(nc, "sG", async_=True)

    def vsep():
        S.on("vector", lambda e: (e.engine_nop(), e.engine_nop(),
                                  e.engine_nop())[-1])

    def ln_stats(src_ap, waits=()):
        S.on("vector", lambda e: e.bn_stats(out=bnst[:, 0:6],
                                            in_=src_ap[:, 0:512]), waits=waits)
        S.on("vector", lambda e: e.bn_stats(out=bnst[:, 6:12],
                                            in_=src_ap[:, 512:1024]))
        vsep()
        S.on("vector", lambda e: e.bn_aggr(out=stat[:, 0:2], in_=bnst[:, 0:12]),
             inc=sV)
        return sV.snap()

    def on_dma(eng, dst, src, waits=()):
        ctr = {"sync": dY, "scalar": dS, "gpsimd": dG}[eng]
        S.on(eng, lambda e, dst=dst, src=src: e.dma_start(out=dst, in_=src),
             waits=waits, inc=ctr, inc_by=16)
        return ctr.snap()

    xr = x_h.rearrange("(n p) d -> p n d", p=128)
    for k in range(4):
        on_dma("gpsimd", y1v[:, 4 * k:4 * k + 4, :], xr[:, 4 * k:4 * k + 4, :])
    ev_x = dG.snap()

    on_dma("gpsimd", idb, id_bf_h.ap())
    on_dma("gpsimd", idf, id_f32_h.ap())
    on_dma("sync", idr, id_f32_h.ap().bitcast(F32R))
    on_dma("sync", wr_sb.rearrange("p (c r) -> p c r", c=DC),
           wr_h.rearrange("(c p) r -> p c r", p=128).bitcast(F32R))
    ev_wrbr = on_dma("sync", br_sb[0:1, :], br_h.bitcast(F32R)[None, :])
    on_dma("gpsimd", beta_sb[0:8, :], bet_h)
    ev_init_g = dG.snap()

    for fn in (lambda e: e.memset(ones_r_bf[0:1, :], 1.0),
               lambda e: e.memset(ones_c_bf[:, 0:1], 1.0),
               lambda e: e.memset(ones_r_f32[0:1, :], 1.0),
               lambda e: e.memset(eps_ln[:, 0:1], EPS_LN),
               lambda e: e.memset(eps_exp[:, 0:1], EPS_EXP),
               lambda e: e.memset(kvd[:, :], 0.0)):
        S.on("vector", fn)
    S.on("vector", lambda e: e.memset(ksc[:, :], 0.0))
    S.on("vector", lambda e: e.tensor_copy(out=ones_r_r[0:1, :],
                                           in_=ones_r_f32[0:1, :]), inc=sV)
    ev_const = sV.snap()

    ev_row_free = None
    for src, dst in ((bk_h, rowk_bf), (bv_h, rowv_bf), (bo_h, rowo_bf)):
        ev_d = on_dma("gpsimd", rowtmp[0:1, :], src[None, :],
                      waits=[ev_row_free])
        S.on("vector", lambda e, dst=dst: e.tensor_copy(out=dst[0:1, :],
                                                        in_=rowtmp[0:1, :]),
             waits=[ev_d], inc=sV)
        ev_row_free = sV.snap()

    ev_bq = on_dma("gpsimd", b32[0:8, :], bq_h.rearrange("(c j) -> c j", j=128))
    S.on("tensor", lambda e: e.transpose(tp_f32[:, 512:520], b32[0:8, :], idf[0:8, 0:8]),
         waits=[ev_bq, ev_init_g, ev_const], inc=sT)
    S.on("vector", lambda e: e.tensor_copy(out=bqf[:, :], in_=tp_f32[:, 512:520]),
         waits=[sT.snap()], inc=sV)
    vsep()
    S.on("vector", lambda e: e.tensor_scalar(out=nbqf[:, :], in0=bqf[:, :],
                                             scalar1=-1.0, scalar2=None,
                                             op0=ALU.mult), inc=sV)
    ev_bqf = sV.snap()
    ev_pt_free = sV.snap()

    ev_stage_free = [None, None]

    def stage_weight(w_hbm, dst3, prev_free_ev):
        nonlocal ev_stage_free
        last_cast = None
        for p in range(8):
            ev_d = on_dma("sync", wstage[p % 2][:, :],
                          w_hbm.rearrange("(c p) d -> p c d", p=128)[:, p, :],
                          waits=[ev_stage_free[p % 2]])
            S.on("vector", lambda e, p=p: e.tensor_copy(out=dst3[:, p, :],
                                                        in_=wstage[p % 2][:, :]),
                 waits=[ev_d, (prev_free_ev if p == 0 else None)], inc=sV)
            ev_stage_free[p % 2] = sV.snap()
            last_cast = sV.snap()
        return last_cast

    ev_wk = stage_weight(wk_h, wkb3, None)
    ev_wv = stage_weight(wv_h, wvb3, None)

    ev_kp_free = None
    ev_vp_free = None
    ev_ktv_free = None
    ev_vtv_free = None
    ev_lt_free = None
    ev_sa_free = None
    ev_se_free = None
    ev_kvaccd = None
    for tt in range(NT):
        xt = y1v[:, tt, :]
        ev_stats = ln_stats(xt, waits=[ev_x, ev_const])
        S.on("scalar", lambda e: e.activation(out=stat[:, 2:3], in_=stat[:, 1:2],
                                              func=AF.Ln, bias=eps_ln[:, 0:1]),
             waits=[ev_stats])
        S.on("scalar", lambda e: e.activation(out=stat[:, 3:4], in_=stat[:, 2:3],
                                              func=AF.Exp, scale=-0.5), inc=sS)
        ev_rstd = sS.snap()
        if "st" in taps:
            S.on("vector", lambda e, tt=tt: e.tensor_copy(
                out=sttap[:, 4 * tt:4 * tt + 4], in_=stat[:, 0:4]),
                waits=[ev_rstd])
        S.on("vector", lambda e, xt=xt: e.tensor_scalar(
            out=lt_bf[:, :], in0=xt, scalar1=stat[:, 0:1], scalar2=stat[:, 3:4],
            op0=ALU.subtract, op1=ALU.mult), waits=[ev_rstd, ev_lt_free], inc=sV)
        ev_lt = sV.snap()
        if "lt" in taps and tt == 3:
            S.on("vector", lambda e: e.tensor_copy(out=at_bf[:, :], in_=lt_bf[:, :]),
                 inc=sV)
            on_dma("gpsimd", tap_h["lt"], at_bf[:, :], waits=[sV.snap()])
        for c in range(DC):
            S.on("tensor", lambda e, c=c: e.transpose(
                tp_bf[:, 1024 + c * 128:1024 + (c + 1) * 128],
                lt_bf[:, c * 128:(c + 1) * 128], idb),
                waits=([ev_lt, ev_pt_free] if c == 0 else []),
                inc=(sT if c == DC - 1 else None))
        ev_t = sT.snap()
        ev_lt_free = ev_t
        S.on("vector", lambda e, tt=tt: e.tensor_copy(
            out=ln1f3[:, :, tt * 128:(tt + 1) * 128],
            in_=tp_bf[:, 1024:2048].rearrange("p (c t) -> p c t", c=DC)),
            waits=[ev_t], inc=sV)
        ev_pt_free = sV.snap()
        ev_ln1 = ev_pt_free

        for half in range(2):
            cols = slice(half * 512, half * 512 + 512)
            S.on("tensor", lambda e, cols=cols: e.matmul(
                PZ0[:, cols], ones_r_bf[0:1, :], rowk_bf[0:1, cols],
                start=True, stop=False),
                waits=([ev_ln1, ev_wk, ev_kp_free] if half == 0 else []))
            for c in range(DC):
                S.on("tensor", lambda e, c=c, cols=cols, tt=tt: e.matmul(
                    PZ0[:, cols], ln1f3[:, c, tt * 128:(tt + 1) * 128],
                    wkb3[:, c, cols], start=False, stop=(c == DC - 1)),
                    inc=(sT if (half == 1 and c == DC - 1) else None))
        ev_kmm = sT.snap()
        S.on("scalar", lambda e: e.activation(out=sa[:, 0:512], in_=PZ0[:, 0:512],
                                              func=AF.Relu, scale=-1.0),
             waits=[ev_kmm, ev_sa_free])
        S.on("scalar", lambda e: e.activation(out=sa[:, 0:512], in_=sa[:, 0:512],
                                              func=AF.Exp, scale=-1.0))
        S.on("scalar", lambda e: e.activation(out=se[:, 0:512], in_=PZ0[:, 512:1024],
                                              func=AF.Relu, scale=-1.0),
             waits=[ev_se_free])
        S.on("scalar", lambda e: e.activation(out=se[:, 0:512], in_=se[:, 0:512],
                                              func=AF.Exp, scale=-1.0), inc=sS)
        ev_phi = sS.snap()
        S.on("vector", lambda e: e.scalar_tensor_tensor(
            out=ktv[:, 0:512], in0=PZ0[:, 0:512], scalar=0.0, in1=sa[:, 0:512],
            op0=ALU.max, op1=ALU.add), waits=[ev_phi, ev_ktv_free])
        S.on("vector", lambda e: e.scalar_tensor_tensor(
            out=ktv[:, 512:1024], in0=PZ0[:, 512:1024], scalar=0.0, in1=se[:, 0:512],
            op0=ALU.max, op1=ALU.add), inc=sV)
        ev_kt = sV.snap()
        ev_kp_free = ev_kt
        ev_sa_free = ev_kt
        ev_se_free = ev_kt

        for half in range(2):
            cols = slice(half * 512, half * 512 + 512)
            S.on("tensor", lambda e, cols=cols: e.matmul(
                PZ1[:, cols], ones_r_bf[0:1, :], rowv_bf[0:1, cols],
                start=True, stop=False),
                waits=([ev_ln1, ev_wv, ev_vp_free] if half == 0 else []))
            for c in range(DC):
                S.on("tensor", lambda e, c=c, cols=cols, tt=tt: e.matmul(
                    PZ1[:, cols], ln1f3[:, c, tt * 128:(tt + 1) * 128],
                    wvb3[:, c, cols], start=False, stop=(c == DC - 1)),
                    inc=(sT if (half == 1 and c == DC - 1) else None))
        ev_vmm = sT.snap()
        S.on("vector", lambda e: e.tensor_copy(out=vtv[:, :], in_=PZ1[:, :]),
             waits=[ev_vmm, ev_vtv_free], inc=sV)
        ev_vt = sV.snap()
        ev_vp_free = ev_vt

        wl0 = [ev_kt, ev_vt]
        if ev_kvaccd is not None:
            wl0.append(ev_kvaccd)
        for h in range(H):
            c, j = h // 2, h % 2
            S.on("tensor", lambda e, c=c, j=j, h=h, tt=tt: e.matmul(
                PR0[64 * j:64 * j + 64, 64 * c:64 * c + 64],
                ktv[:, 64 * h:64 * h + 64], vtv[:, 64 * h:64 * h + 64],
                start=True, stop=True, skip_group_check=True),
                waits=(wl0 if h == 0 else []))
        S.on("tensor", lambda e, tt=tt: e.matmul(
            PR0[0:1, 512:1024], ones_c_bf[:, 0:1], ktv[:, 0:512],
            start=(tt == 0), stop=(tt == NT - 1), skip_group_check=True))
        S.on("tensor", lambda e, tt=tt: e.matmul(
            PR1[0:1, 0:512], ones_c_bf[:, 0:1], ktv[:, 512:1024],
            start=(tt == 0), stop=(tt == NT - 1), skip_group_check=True), inc=sT)
        ev_kvmm = sT.snap()
        if tt == 0:
            S.on("vector", lambda e: e.tensor_copy(out=kvacc[:, :],
                                                   in_=PR0[:, 0:512]),
                 waits=[ev_kvmm], inc=sV)
        else:
            S.on("vector", lambda e: e.tensor_add(out=kvacc[:, :],
                                                  in0=kvacc[:, :],
                                                  in1=PR0[:, 0:512]),
                 waits=[ev_kvmm], inc=sV)
        ev_kvaccd = sV.snap()
        ev_ktv_free = sT.snap()
        ev_vtv_free = sT.snap()
    ev_kv_done = sT.snap()
    if "ln1" in taps:
        on_dma("gpsimd", tap_h["ln1"], ln1f[:, :], waits=[ev_pt_free])
    if "st" in taps:
        on_dma("gpsimd", tap_h["st"], sttap[:, :], waits=[sV.snap()])


    S.on("vector", lambda e: e.tensor_copy(out=sa[:, 0:512], in_=kvacc[:, :]),
         waits=[ev_kv_done])
    S.on("vector", lambda e: e.tensor_copy(out=se[0:1, 0:512], in_=PR0[0:1, 512:1024]))
    S.on("vector", lambda e: e.tensor_copy(out=se[0:1, 512:1024], in_=PR1[0:1, 0:512]),
         inc=sV)
    ev_ksrow = sV.snap()
    for j in range(8):
        S.on("tensor", lambda e, j=j: e.transpose(
            tp_f32[:, 512 + j:513 + j], se[0:1, j * 128:(j + 1) * 128],
            idf[0:1, 0:1]),
            waits=([ev_ksrow, ev_pt_free] if j == 0 else []),
            inc=(sT if j == 7 else None))
    ev_kst = sT.snap()
    S.on("vector", lambda e: e.tensor_copy(out=sa[:, 512:520], in_=tp_f32[:, 512:520]),
         waits=[ev_kst], inc=sV)
    ev_ccin = sV.snap()
    ev_pt_free = ev_ccin
    ev_ccd = on_dma("gpsimd", cc_in_h.ap(), sa[:, 0:520], waits=[ev_ccin])
    S.on("gpsimd", lambda e: e.collective_compute(
        "AllReduce", ALU.add, replica_groups=[list(range(NCORES))],
        ins=[cc_in_h.ap().opt()], outs=[cc_out_h.ap().opt()]),
        waits=[ev_ccd], inc=sG)
    ev_cc = sG.snap()
    ev_aro = on_dma("gpsimd", aro[:, :], cc_out_h.ap(), waits=[ev_cc])

    ev_wq = stage_weight(wq_h, wkb3, ev_kv_done)
    ev_wo = stage_weight(wo_h, wvb3, ev_kv_done)
    qslots = [PZ0[:, 0:512], PZ0[:, 512:1024], PZ1[:, 0:512], PZ1[:, 512:1024]]
    ev_qslot_free = [ev_kp_free, ev_kp_free, ev_vp_free, ev_vp_free]
    qt1 = se
    qt2 = sa
    ev_qt1_free = [ev_kst, ev_kst]
    ev_qt2_free = ev_ccd
    job = 0
    for t4 in range(4):
        for dc in range(DC):
            sl = qslots[job % 4]
            for c in range(DC):
                S.on("tensor", lambda e, c=c, dc=dc, t4=t4, sl=sl: e.matmul(
                    sl, wkb3[:, c, dc * 128:(dc + 1) * 128],
                    ln1f3[:, c, t4 * 512:(t4 + 1) * 512],
                    start=(c == 0), stop=(c == DC - 1)),
                    waits=([ev_wq, ev_bqf, ev_qslot_free[job % 4]] if c == 0 else []),
                    inc=(sT if c == DC - 1 else None))
            ev_qmm = sT.snap()
            q1 = qt1[:, (job % 2) * 512:(job % 2) * 512 + 512]
            S.on("scalar", lambda e, q1=q1, dc=dc, sl=sl: e.activation(
                out=q1, in_=sl, func=AF.Relu, scale=-1.0, bias=nbqf[:, dc:dc + 1]),
                waits=[ev_qmm, ev_qt1_free[job % 2]])
            S.on("scalar", lambda e, q1=q1: e.activation(
                out=q1, in_=q1, func=AF.Exp, scale=-1.0), inc=sS)
            ev_qphi = sS.snap()
            S.on("vector", lambda e, dc=dc, sl=sl: e.tensor_scalar(
                out=qt2[:, 0:512], in0=sl, scalar1=bqf[:, dc:dc + 1], scalar2=0.0,
                op0=ALU.add, op1=ALU.max), waits=[ev_qphi, ev_qt2_free])
            S.on("vector", lambda e, q1=q1, dc=dc, t4=t4: e.tensor_add(
                out=q3[:, dc, t4 * 512:(t4 + 1) * 512], in0=qt2[:, 0:512], in1=q1),
                inc=sV)
            ev_qev = sV.snap()
            ev_qslot_free[job % 4] = ev_qev
            ev_qt1_free[job % 2] = ev_qev
            ev_qt2_free = None
            job += 1
    ev_q_done = sV.snap()

    ev_qtap = None
    if "q" in taps:
        ev_qtap = on_dma("gpsimd", tap_h["q"], q3r[:, :], waits=[ev_q_done])
    if "aro" in taps:
        on_dma("gpsimd", tap_h["aro"], aro[:, :], waits=[ev_aro])

    for h in range(H):
        c, j = h // 2, h % 2
        S.on("vector", lambda e, c=c, j=j: e.tensor_copy(
            out=kvd3[64 * j:64 * j + 64, c, 64 * j:64 * j + 64],
            in_=aro[64 * j:64 * j + 64, 64 * c:64 * c + 64]),
            waits=([ev_aro] if h == 0 else []))
    for c in range(DC):
        S.on("vector", lambda e, c=c: e.tensor_copy(
            out=ksc[0:64, 2 * c:2 * c + 1], in_=aro[0:64, 512 + c:513 + c]))
        S.on("vector", lambda e, c=c: e.tensor_copy(
            out=ksc[64:128, 2 * c + 1:2 * c + 2], in_=aro[64:128, 512 + c:513 + c]),
            inc=(sV if c == DC - 1 else None))
    ev_kvd = sV.snap()

    ev_qtap = ev_qtap if "q" in taps else None
    ev_pz_free = ev_q_done
    ev_pr_free = None
    ev_at_free = None
    ev_tb_free = ev_q_done
    for tt in range(NT):
        for c in range(DC):
            S.on("tensor", lambda e, c=c, tt=tt: e.matmul(
                PZ0[:, c * 128:(c + 1) * 128], q3[:, c, tt * 128:(tt + 1) * 128],
                kvd3[:, c, :], start=True, stop=True,
                skip_group_check=True),
                waits=([ev_kvd, ev_q_done, ev_pz_free] if c == 0 else []))
        for c in range(DC):
            S.on("tensor", lambda e, c=c, tt=tt: e.matmul(
                PZ1[:, 2 * c:2 * c + 2], q3[:, c, tt * 128:(tt + 1) * 128],
                ksc[:, 2 * c:2 * c + 2], start=True, stop=True,
                skip_group_check=True), inc=(sT if c == DC - 1 else None))
        ev_attmm = sT.snap()
        S.on("vector", lambda e: e.tensor_scalar(out=nrm[:, :], in0=PZ1[:, 0:16],
                                                 scalar1=EPS_ATT, scalar2=None,
                                                 op0=ALU.max), waits=[ev_attmm])
        vsep()
        S.on("vector", lambda e: e.reciprocal(out=rnn[:, :], in_=nrm[:, :]))
        vsep()
        for hh in range(H):
            S.on("vector", lambda e, hh=hh: e.tensor_scalar(
                out=at_bf[:, 64 * hh:64 * hh + 64], in0=PZ0[:, 64 * hh:64 * hh + 64],
                scalar1=rnn[:, hh:hh + 1], scalar2=None, op0=ALU.mult),
                waits=([ev_at_free] if hh == 0 else []),
                inc=(sV if hh == H - 1 else None))
        ev_div = sV.snap()
        ev_pz_free = ev_div
        for c in range(DC):
            S.on("tensor", lambda e, c=c: e.transpose(
                tp_bf[:, 1024 + c * 128:1024 + (c + 1) * 128],
                at_bf[:, c * 128:(c + 1) * 128], idb),
                waits=([ev_div, ev_pt_free] if c == 0 else []),
                inc=(sT if c == DC - 1 else None))
        ev_att = sT.snap()
        ev_at_free = ev_att
        S.on("vector", lambda e, tt=tt: e.tensor_copy(
            out=q3[:, :, tt * 128:(tt + 1) * 128],
            in_=tp_bf[:, 1024:2048].rearrange("p (c t) -> p c t", c=DC)),
            waits=[ev_att, ev_qtap], inc=sV)
        ev_afm = sV.snap()
        ev_pt_free = ev_afm

        for half in range(2):
            cols = slice(half * 512, half * 512 + 512)
            S.on("tensor", lambda e, cols=cols: e.matmul(
                PR0[:, cols], ones_r_bf[0:1, :], rowo_bf[0:1, cols],
                start=True, stop=False),
                waits=([ev_afm, ev_wo, ev_pr_free] if half == 0 else []))
            for c in range(DC):
                S.on("tensor", lambda e, c=c, cols=cols, tt=tt: e.matmul(
                    PR0[:, cols], q3[:, c, tt * 128:(tt + 1) * 128],
                    wvb3[:, c, cols], start=False, stop=(c == DC - 1)),
                    inc=(sT if (half == 1 and c == DC - 1) else None))
        ev_prj = sT.snap()
        ln_stats(PR0, waits=[ev_prj])
        S.on("scalar", lambda e: e.activation(out=stat[:, 2:3], in_=stat[:, 1:2],
                                              func=AF.Ln, bias=eps_ln[:, 0:1]),
             waits=[sV.snap()])
        S.on("scalar", lambda e: e.activation(out=stat[:, 3:4], in_=stat[:, 2:3],
                                              func=AF.Exp, scale=-0.5), inc=sS)
        S.on("vector", lambda e: e.tensor_scalar(
            out=se[:, :], in0=PR0[:, :], scalar1=stat[:, 0:1], scalar2=stat[:, 3:4],
            op0=ALU.subtract, op1=ALU.mult), waits=[sS.snap(), ev_tb_free])
        S.on("vector", lambda e, tt=tt: e.tensor_add(
            out=y1v[:, tt, :], in0=y1v[:, tt, :], in1=se[:, :]), inc=sV)
        ev_pr_free = sV.snap()
        ev_tb_free = sV.snap()
    ev_y1 = sV.snap()
    ev_rowo_used = sT.snap()
    ev_q3_free = sT.snap()

    if "y1" in taps:
        on_dma("gpsimd", tap_h["y1"], y1[:, :], waits=[ev_y1])

    ev_spill = None
    ev_hfree = None
    ev_zb_free = [ev_pz_free, ev_pz_free]
    ev_rp_free = [None, None, ev_pr_free, None]
    ev_zev = [None, None]
    ev_wgi_free = [None, None]
    ev_wstg_free = [None, None]
    ev_wobf_free = [None, None]
    ev_bias_free = [None, None]
    ev_gbc_free = None
    ev_xf_free = ev_q3_free
    ev_xtm_free = ev_q3_free
    ev_c_free = None
    ev_ltr_free = None
    ev_t1_free = None
    ev_ring_free = None
    ev_final_done = None
    ev_esc_free = None
    ev_unionU = []

    for hf in range(2):
        ev_xf_ready = []
        for i in range(NTH):
            gtt = hf * NTH + i
            if hf == 0:
                yt = y1v[:, gtt, :]
                wl_y = [ev_y1]
            else:
                ev_rl = on_dma("gpsimd", ringA[:, :], spill3[:, gtt, :],
                               waits=[ev_ring_free])
                yt = ringA[:, :]
                wl_y = [ev_rl]
            ev_st = ln_stats(yt, waits=wl_y)
            vsep()
            S.on("vector", lambda e, i=i: e.tensor_copy(
                out=ln2st[:, 2 * i:2 * i + 1], in_=stat[:, 0:1]), inc=sV)
            ev_st = sV.snap()
            S.on("scalar", lambda e: e.activation(out=stat[:, 2:3], in_=stat[:, 1:2],
                                                  func=AF.Ln, bias=eps_ln[:, 0:1]),
                 waits=[ev_st])
            S.on("scalar", lambda e: e.activation(out=stat[:, 3:4], in_=stat[:, 2:3],
                                                  func=AF.Exp, scale=-0.5), inc=sS)
            ev_rstd2 = sS.snap()
            S.on("vector", lambda e, yt=yt: e.tensor_scalar(
                out=ltr[:, :], in0=yt, scalar1=stat[:, 0:1], scalar2=stat[:, 3:4],
                op0=ALU.subtract, op1=ALU.mult), waits=[ev_rstd2, ev_ltr_free])
            S.on("vector", lambda e, yt=yt, i=i: e.tensor_scalar(
                out=xtm3[:, i, :], in0=yt, scalar1=stat[:, 0:1], scalar2=stat[:, 3:4],
                op0=ALU.subtract, op1=ALU.mult),
                waits=[ev_xtm_free] if i == 0 else [], inc=sV)
            ev_ltr = sV.snap()
            ev_ring_free = sV.snap()
            for rnd in range(2):
                for j in range(4):
                    c = rnd * 4 + j
                    S.on("tensor", lambda e, c=c, j=j: e.transpose(
                        tp_f32r[:, 512 + j * 128:512 + (j + 1) * 128],
                        ltr[:, c * 128:(c + 1) * 128], idr),
                        waits=([ev_ltr, ev_pt_free,
                                (ev_xf_free if (i == 0 and rnd == 0) else None)]
                               if j == 0 else []),
                        inc=(sT if j == 3 else None))
                ev_tr = sT.snap()
                S.on("vector", lambda e, rnd=rnd, i=i: e.tensor_copy(
                    out=xf3[:, rnd * 4:rnd * 4 + 4, i * 128:(i + 1) * 128],
                    in_=tp_f32r[:, 512:1024].rearrange("p (c t) -> p c t", c=4)),
                    waits=[ev_tr], inc=sV)
                ev_pt_free = sV.snap()
            ev_ltr_free = sT.snap()
            ev_xf_ready.append(ev_pt_free)

            S.on("tensor", lambda e: e.matmul(
                PZ0[:, 0:8], ones_r_r[0:1, :], br_sb[0:1, :],
                start=True, stop=False),
                waits=[ev_pt_free, ev_wrbr, ev_zb_free[0]])
            for c in range(DC):
                S.on("tensor", lambda e, c=c, i=i: e.matmul(
                    PZ0[:, 0:8], xf3[:, c, i * 128:(i + 1) * 128], wr3[:, c, :],
                    start=False, stop=(c == DC - 1)),
                    inc=(sT if c == DC - 1 else None))
            ev_lg = sT.snap()
            S.on("vector", lambda e, i=i: e.tensor_copy(
                out=lgts[:, i * E:(i + 1) * E], in_=PZ0[:, 0:8]),
                waits=[ev_lg], inc=sV)
            ev_zb_free[0] = sV.snap()
        ev_xtm_done = sV.snap()
        for i in range(NTH):
            S.on("vector", lambda e, i=i: e.reduce_max(
                out=m1s[:, i:i + 1], in_=lgts[:, i * E:(i + 1) * E], axis=AX.X))
        for i in range(NTH):
            S.on("vector", lambda e, i=i: e.tensor_scalar(
                out=eq1v[:, i * E:(i + 1) * E], in0=lgts[:, i * E:(i + 1) * E],
                scalar1=m1s[:, i:i + 1], scalar2=None, op0=ALU.is_equal))
        for i in range(NTH):
            S.on("vector", lambda e, i=i: e.scalar_tensor_tensor(
                out=mkts[:, i * E:(i + 1) * E], in0=eq1v[:, i * E:(i + 1) * E],
                scalar=-1e30, in1=lgts[:, i * E:(i + 1) * E],
                op0=ALU.mult, op1=ALU.add))
        for i in range(NTH):
            S.on("vector", lambda e, i=i: e.reduce_max(
                out=m2s[:, i:i + 1], in_=mkts[:, i * E:(i + 1) * E], axis=AX.X))
        for i in range(NTH):
            S.on("vector", lambda e, i=i: e.tensor_scalar(
                out=eq2v[:, i * E:(i + 1) * E], in0=lgts[:, i * E:(i + 1) * E],
                scalar1=m2s[:, i:i + 1], scalar2=None, op0=ALU.is_equal))
        for i in range(NTH):
            S.on("vector", lambda e, i=i: e.tensor_sub(
                out=ddv[:, i:i + 1], in0=m1s[:, i:i + 1], in1=m2s[:, i:i + 1]),
                inc=(sV if i == NTH - 1 else None))
        ev_router_v = sV.snap()

        if hf == 0:
            ev_spill = on_dma("gpsimd", spill_h, y1[:, :], waits=[ev_xtm_done])
            ev_hfree = ev_spill

        S.on("scalar", lambda e: e.activation(out=w1v[:, :], in_=ddv[:, :],
                                              func=AF.Sigmoid), waits=[ev_router_v])
        S.on("scalar", lambda e: e.activation(out=w2v[:, :], in_=ddv[:, :],
                                              func=AF.Sigmoid, scale=-1.0), inc=sS)
        ev_w = sS.snap()
        for i in range(NTH):
            S.on("vector", lambda e, i=i: e.tensor_scalar(
                out=c3[:, i, :], in0=eq1v[:, i * E:(i + 1) * E],
                scalar1=w1v[:, i:i + 1], scalar2=None, op0=ALU.mult),
                waits=[ev_w, ev_c_free] if i == 0 else [])
        for i in range(NTH):
            S.on("vector", lambda e, i=i: e.scalar_tensor_tensor(
                out=c3[:, i, :], in0=eq2v[:, i * E:(i + 1) * E],
                scalar=w2v[:, i:i + 1], in1=c3[:, i, :],
                op0=ALU.mult, op1=ALU.add), inc=sV)
        ev_c = sV.snap()
        if "c" in taps and hf == 0:
            on_dma("gpsimd", tap_h["c"], c_sb[:, :], waits=[ev_c])
        if "c1" in taps and hf == 1:
            on_dma("gpsimd", tap_h["c1"], c_sb[:, :], waits=[ev_c])
        if "xtm1" in taps and hf == 1:
            on_dma("gpsimd", tap_h["xtm1"], xtm[:, :], waits=[ev_xtm_done])

        ev_oainit = []
        for i in range(NTH):
            S.on("tensor", lambda e, i=i: e.transpose(
                tp_f32[0:8, 512:640], c3[:, i, :], idf),
                waits=[ev_c, ev_pt_free], inc=sT)
            ev_ct = sT.snap()
            S.on("vector", lambda e: e.tensor_copy(out=cT_sb[0:8, :],
                                                   in_=tp_f32[0:8, 512:640]),
                 waits=[ev_ct], inc=sV)
            ev_ctc = sV.snap()
            ev_pt_free = ev_ctc
            for half in range(2):
                cols = slice(half * 512, half * 512 + 512)
                S.on("tensor", lambda e, cols=cols: e.matmul(
                    PZ1[:, cols], cT_sb[0:8, :], beta_sb[0:8, cols],
                    start=True, stop=True),
                    waits=([ev_ctc, ev_zb_free[1]] if half == 0 else []),
                    inc=(sT if half == 1 else None))
            ev_cb = sT.snap()
            wl = [ev_cb]
            if i == 0:
                wl.append(ev_q3_free)
            if hf == 1:
                wl.append(ev_oa_free[i])
            S.on("vector", lambda e, i=i: e.tensor_copy(out=oa3[:, i, :],
                                                        in_=PZ1[:, :]),
                 waits=wl, inc=sV)
            ev_oainit.append(sV.snap())
            ev_zb_free[1] = sV.snap()

        for ex in range(E):
            br_ = ex % 2
            ev_bi = on_dma("gpsimd", b32[0:32, :],
                           bi_h[ex].rearrange("(r j) -> r j", j=128),
                           waits=[ev_bias_free[br_]])
            S.on("tensor", lambda e: e.transpose(tp_f32[:, 512:544], b32[0:32, :],
                                                 idf[0:32, 0:32]),
                 waits=[ev_bi, ev_pt_free, ev_rp_free[3]], inc=sT)
            ev_bt = sT.snap()
            S.on("vector", lambda e, br_=br_: e.tensor_copy(out=bif[br_][:, :],
                                                            in_=tp_f32[:, 512:544]),
                 waits=[ev_bt], inc=sV)
            ev_bic = sV.snap()
            ev_bg = on_dma("gpsimd", b32[0:32, :],
                           bg_h[ex].rearrange("(r j) -> r j", j=128), waits=[ev_bic])
            S.on("tensor", lambda e: e.transpose(tp_f32[:, 512:544], b32[0:32, :],
                                                 idf[0:32, 0:32]),
                 waits=[ev_bg], inc=sT)
            ev_bt2 = sT.snap()
            S.on("vector", lambda e, br_=br_: e.tensor_copy(out=bgf[br_][:, :],
                                                            in_=tp_f32[:, 512:544]),
                 waits=[ev_bt2], inc=sV)
            ev_bgc = sV.snap()
            ev_pt_free = ev_bgc
            ev_bias_free[br_] = ev_bgc

            ev_bo = on_dma("gpsimd", rowtmp[0:1, :], boe_h[ex][None, :],
                           waits=[ev_row_free])
            S.on("vector", lambda e: e.tensor_copy(out=rowo_bf[0:1, :],
                                                   in_=rowtmp[0:1, :]),
                 waits=[ev_bo, ev_rowo_used], inc=sV)
            ev_rowo = sV.snap()
            ev_gam = on_dma("gpsimd", rowtmp[0:1, :], gam_h[ex][None, :],
                            waits=[ev_rowo])
            for half in range(2):
                cols = slice(half * 512, half * 512 + 512)
                S.on("tensor", lambda e, cols=cols: e.matmul(
                    PR0[:, cols], ones_r_f32[0:1, :], rowtmp[0:1, cols],
                    start=True, stop=True),
                    waits=([ev_gam, ev_rp_free[2]] if half == 0 else []),
                    inc=(sT if half == 1 else None))
            ev_gmm = sT.snap()
            ev_row_free = sT.snap()
            S.on("vector", lambda e: e.tensor_copy(out=gbc[:, :], in_=PR0[:, :]),
                 waits=[ev_gmm, ev_gbc_free], inc=sV)
            ev_gbc = sV.snap()
            ev_rp_free[2] = ev_gbc

            def wblock_dma(w_hbm, b, waits):
                return on_dma("sync", wgi3[b % 2][:, :, :],
                              w_hbm[ex][:, 256 * b:256 * b + 256]
                              .rearrange("(c p) f -> p c f", p=128)
                              .bitcast(F32R), waits=waits)

            ev_blk = [None] * 16
            ev_rdy = [None, None]
            for b in range(2):
                wl = [ev_wgi_free[b]]
                if hf == 0 and ex == 0:
                    wl += [ev_q3_free, ev_y1]
                if b == 1:
                    wl += list(ev_unionU)
                    if ex == 0:
                        wl.append(ev_xtm_done)
                ev_rdy[b] = wblock_dma(wg_h, b, wl)
            for b in range(16):
                for s2 in range(2):
                    fc = 2 * b + s2
                    zb = PZ0 if fc % 2 == 0 else PZ1
                    wl = [ev_rdy[b % 2], ev_bgc, ev_zb_free[fc % 2], ev_zev[fc % 2]]
                    if fc < 2:
                        wl.append(ev_hfree)
                    for half in range(2):
                        cols = slice(half * 512, half * 512 + 512)
                        for c in range(DC):
                            S.on("tensor", lambda e, c=c, s2=s2, b=b, cols=cols, zb=zb: e.matmul(
                                zb[:, cols],
                                wgi3[b % 2][:, c, s2 * 128:(s2 + 1) * 128],
                                xf3[:, c, cols],
                                start=(c == 0), stop=(c == DC - 1)),
                                waits=(wl if (half == 0 and c == 0) else []),
                                inc=(sT if (half == 1 and c == DC - 1) else None))
                    ev_zmm = sT.snap()
                    S.on("scalar", lambda e, fc=fc, zb=zb, br_=br_: e.activation(
                        out=hview(fc), in_=zb[:, :], func=AF.Sigmoid,
                        bias=bgf[br_][:, fc:fc + 1]), waits=[ev_zmm], inc=sS)
                    ev_zev[fc % 2] = sS.snap()
                    ev_zb_free[fc % 2] = None
                    if s2 == 1:
                        ev_blk[b] = sT.snap()
                if b + 2 < 16:
                    ev_rdy[b % 2] = wblock_dma(wg_h, b + 2, [ev_blk[b]])
            ev_wgi_free = [ev_blk[14], ev_blk[15]]

            ev_rdy = [None, None]
            for b in range(2):
                ev_rdy[b] = wblock_dma(wi_h, b, [ev_wgi_free[b]])
            ev_hm = None
            for b in range(16):
                for s2 in range(2):
                    fc = 2 * b + s2
                    zb = PZ0 if fc % 2 == 0 else PZ1
                    wl = [ev_rdy[b % 2], ev_bic, ev_zev[fc % 2]]
                    for half in range(2):
                        cols = slice(half * 512, half * 512 + 512)
                        for c in range(DC):
                            S.on("tensor", lambda e, c=c, s2=s2, b=b, cols=cols, zb=zb: e.matmul(
                                zb[:, cols],
                                wgi3[b % 2][:, c, s2 * 128:(s2 + 1) * 128],
                                xf3[:, c, cols],
                                start=(c == 0), stop=(c == DC - 1)),
                                waits=(wl if (half == 0 and c == 0) else []),
                                inc=(sT if (half == 1 and c == DC - 1) else None))
                    ev_zmm = sT.snap()
                    S.on("scalar", lambda e, zb=zb, br_=br_, fc=fc: e.activation(
                        out=t1_bf[:, :], in_=zb[:, :], func=AF.Gelu,
                        bias=bif[br_][:, fc:fc + 1]),
                        waits=[ev_zmm, ev_t1_free], inc=sS)
                    ev_gel = sS.snap()
                    S.on("vector", lambda e, fc=fc: e.tensor_mul(
                        out=hview(fc), in0=t1_bf[:, :], in1=hview(fc)),
                        waits=[ev_gel], inc=sV)
                    ev_hm = sV.snap()
                    ev_t1_free = ev_hm
                    ev_zev[fc % 2] = ev_hm
                    if s2 == 1:
                        ev_blk[b] = sT.snap()
                if b + 2 < 16:
                    ev_rdy[b % 2] = wblock_dma(wi_h, b + 2, [ev_blk[b]])
            ev_wgi_free = [ev_blk[14], ev_blk[15]]
            ev_h_done = ev_hm

            if "h01" in taps and hf == 0 and ex == 0:
                S.on("vector", lambda e: e.tensor_copy(out=esc[:, :], in_=hview(0)),
                     waits=[ev_h_done], inc=sV)
                on_dma("gpsimd", tap_h["h01"].rearrange("p (a t) -> p a t", a=2)[:, 0, :],
                       esc[:, :], waits=[sV.snap()])
                S.on("vector", lambda e: e.tensor_copy(out=esc[:, :], in_=hview(16)),
                     waits=[(dG, dG.n)], inc=sV)
                on_dma("gpsimd", tap_h["h01"].rearrange("p (a t) -> p a t", a=2)[:, 1, :],
                       esc[:, :], waits=[sV.snap()])

            PSUB = [PZ0, PZ1, PR0, PR1]
            for sg in range(2):
                ev_started = []
                for sub in range(4):
                    i = 4 * sg + sub
                    pb = PSUB[sub]
                    if sub < 2:
                        fv = ev_zev[sub] if sg == 0 else ev_rp_free[sub]
                    else:
                        fv = ev_rp_free[sub]
                    extra = ev_pt_free if sub == 3 else None
                    for half in range(2):
                        cols = slice(half * 512, half * 512 + 512)
                        S.on("tensor", lambda e, pb=pb, cols=cols: e.matmul(
                            pb[:, cols], ones_r_bf[0:1, :], rowo_bf[0:1, cols],
                            start=True, stop=False),
                            waits=([ev_h_done, fv, extra, ev_rowo, ev_xtm_done]
                                   if half == 0 else []))
                        S.on("tensor", lambda e, pb=pb, cols=cols, i=i: e.matmul(
                            pb[:, cols], idb, xtm3[:, i, cols],
                            start=False, stop=False),
                            inc=(sT if half == 1 else None))
                    ev_started.append(sT.snap())
                ev_rowo_used = sT.snap()
                for p in range(16):
                    qeng = "sync" if p % 2 == 0 else "gpsimd"
                    wl = [ev_wstg_free[p % 2]]
                    if sg == 0 and p == 0:
                        wl.append(ev_wgi_free[1])
                    if hf == 0 and ex == 0 and sg == 0 and p == 1:
                        wl.append(ev_q3_free)
                    ev_wd = on_dma(qeng,
                                   wstg[p % 2].rearrange("p (c d) -> p c d", c=2),
                                   woe_h[ex][256 * p:256 * p + 256, :]
                                   .rearrange("(c p2) d -> p2 c d", p2=128),
                                   waits=wl)
                    S.on("vector", lambda e, p=p: e.tensor_copy(
                        out=wobf[p % 2][:, :], in_=wstg[p % 2][:, :]),
                        waits=[ev_wd, ev_wobf_free[p % 2]], inc=sV)
                    ev_cast = sV.snap()
                    ev_wstg_free[p % 2] = ev_cast
                    for s2 in range(2):
                        fc = 2 * p + s2
                        for sub in range(4):
                            pb = PSUB[sub]
                            i = 4 * sg + sub
                            for half in range(2):
                                cols = slice(half * 512, half * 512 + 512)
                                S.on("tensor",
                                     lambda e, pb=pb, fc=fc, s2=s2, i=i, p=p, cols=cols: e.matmul(
                                         pb[:, cols],
                                         hview(fc)[:, i * 128:(i + 1) * 128],
                                         wobf3[p % 2][:, s2, cols],
                                         start=False, stop=(fc == FC - 1)),
                                     waits=([ev_cast] if (s2 == 0 and sub == 0
                                                          and half == 0) else []),
                                     inc=(sT if (sub == 3 and half == 1) else None))
                    ev_wobf_free[p % 2] = sT.snap()
                ev_rmm = sT.snap()
                for sub in range(4):
                    i = 4 * sg + sub
                    pb = PSUB[sub]
                    ev_st = ln_stats(pb, waits=[ev_rmm])
                    S.on("scalar", lambda e: e.activation(out=stat[:, 2:3],
                                                          in_=stat[:, 1:2],
                                                          func=AF.Ln,
                                                          bias=eps_exp[:, 0:1]),
                         waits=[ev_st])
                    S.on("scalar", lambda e: e.activation(out=stat[:, 3:4],
                                                          in_=stat[:, 2:3],
                                                          func=AF.Exp, scale=-0.5),
                         inc=sS)
                    ev_rstd = sS.snap()
                    S.on("vector", lambda e, i=i, ex=ex: e.tensor_mul(
                        out=stat[:, 4:5], in0=c3[:, i, ex:ex + 1], in1=stat[:, 3:4]),
                        waits=[ev_rstd])
                    S.on("vector", lambda e, pb=pb: e.scalar_tensor_tensor(
                        out=esc[:, :], in0=pb[:, :], scalar=stat[:, 0:1],
                        in1=gbc[:, :], op0=ALU.subtract, op1=ALU.mult),
                        waits=[ev_gbc, ev_esc_free])
                    S.on("vector", lambda e, i=i: e.scalar_tensor_tensor(
                        out=oa3[:, i, :], in0=esc[:, :], scalar=stat[:, 4:5],
                        in1=oa3[:, i, :], op0=ALU.mult, op1=ALU.add),
                        waits=[ev_oainit[i]], inc=sV)
                    ev_rp_free[sub] = sV.snap()
                    ev_esc_free = sV.snap()
            ev_zb_free = [ev_rp_free[0], ev_rp_free[1]]
            ev_zev = [None, None]
            ev_gbc_free = ev_rp_free[3]
            ev_pt_free = ev_rp_free[3]
            ev_unionU = [ev_wstg_free[0], ev_wstg_free[1],
                         ev_wobf_free[0], ev_wobf_free[1]]

        if "oacc" in taps and hf == 0:
            on_dma("gpsimd", tap_h["oacc"], oa[:, :],
                   waits=[ev_rp_free[0], ev_rp_free[1], ev_rp_free[2],
                          ev_rp_free[3]])
        if "oacc1" in taps and hf == 1:
            on_dma("gpsimd", tap_h["oacc1"], oa[:, :],
                   waits=[ev_rp_free[0], ev_rp_free[1], ev_rp_free[2],
                          ev_rp_free[3]])

        ev_oa_free = {}
        ev_ringB_free = None
        for i in range(NTH):
            gtt = hf * NTH + i
            wl = [ev_ring_free, ev_spill]
            if i == 0:
                wl += [ev_wstg_free[0], ev_wgi_free[1]]
            ev_rl = on_dma("gpsimd", ringA[:, :], spill3[:, gtt, :], waits=wl)
            ev_st = ln_stats(oa3[:, i, :],
                             waits=([ev_rp_free[0], ev_rp_free[1], ev_rp_free[2],
                                     ev_rp_free[3]] if i == 0 else []))
            S.on("scalar", lambda e: e.activation(out=stat[:, 2:3], in_=stat[:, 1:2],
                                                  func=AF.Ln, bias=eps_ln[:, 0:1]),
                 waits=[ev_st])
            S.on("scalar", lambda e: e.activation(out=stat[:, 3:4], in_=stat[:, 2:3],
                                                  func=AF.Exp, scale=-0.5), inc=sS)
            ev_rstd = sS.snap()
            S.on("vector", lambda e, i=i: e.tensor_scalar(
                out=ringB[:, :], in0=oa3[:, i, :], scalar1=stat[:, 0:1],
                scalar2=stat[:, 3:4], op0=ALU.subtract, op1=ALU.mult),
                waits=[ev_rstd, ev_ringB_free])
            S.on("vector", lambda e: e.tensor_add(
                out=ringB[:, :], in0=ringB[:, :], in1=ringA[:, :]),
                waits=[ev_rl], inc=sV)
            ev_ob = sV.snap()
            ev_ring_free = sV.snap()
            ev_oa_free[i] = sV.snap()
            ev_od = on_dma("gpsimd",
                           out_h.rearrange("(n p) d -> p n d", p=128)[:, gtt, :],
                           ringB[:, :], waits=[ev_ob])
            ev_ringB_free = ev_od
        ev_final_done = sV.snap()
        ev_unionU = [ev_ring_free, ev_ringB_free]
        ev_xf_free = ev_wgi_free[1]
        ev_c_free = ev_rp_free[3]
        ev_xtm_free = ev_rowo_used
        ev_ltr_free = ev_t1_free

    S.on("gpsimd", lambda e: e.wait_ge(dG.h, dG.n))
    S.on("sync", lambda e: e.wait_ge(dY.h, dY.n))
    S.on("scalar", lambda e: e.wait_ge(dS.h, dS.n))

    with nc.Block() as block:
        S.build(block)

    return nc, sbuf_hi


def kernel(**inputs):
    x = np.asarray(inputs["L0_attn_inputs_q"])
    xt = np.ascontiguousarray(x.reshape(16384, 1024))
    base = {
        "wq": inputs["L0_attn_Wq"], "bq": inputs["L0_attn_bq"],
        "wk": inputs["L0_attn_Wk"], "bk": inputs["L0_attn_bk"],
        "wv": inputs["L0_attn_Wv"], "bv": inputs["L0_attn_bv"],
        "wo": inputs["L0_attn_Wo"], "bo": inputs["L0_attn_bo"],
        "wr": inputs["L0_ffn_Wr"], "br": inputs["L0_ffn_br"],
        "wi": inputs["L0_ffn_Wi"], "bi": inputs["L0_ffn_bi"],
        "wg": inputs["L0_ffn_Wg"], "bg": inputs["L0_ffn_bg"],
        "woe": inputs["L0_ffn_Wo"], "boe": inputs["L0_ffn_bo"],
        "gam": inputs["L0_ffn_gamma"], "bet": inputs["L0_ffn_beta"],
    }
    base = {k: np.ascontiguousarray(np.asarray(v), dtype=np.float32)
            for k, v in base.items()}
    nc, _ = build_program()
    in_maps = []
    for i in range(NCORES):
        m = dict(base)
        m["x"] = np.ascontiguousarray(xt[i * T:(i + 1) * T])
        in_maps.append(m)
    res = run_bass_kernel_spmd(nc, in_maps, core_ids=list(range(NCORES)))
    out = np.concatenate([np.asarray(res.results[i]["out"])
                          for i in range(NCORES)], axis=0)
    return out.reshape(1, 16384, 1024)


if __name__ == "__main__":
    nc, hi = build_program()
    print("built ok; sbuf bytes/partition:", hi)
    print("instructions:", nc.n_instructions())
